# version 01 (final)
# baseline (speedup 1.0000x reference)
import math
from contextlib import ExitStack

import numpy as np
import ml_dtypes
import concourse.bass as bass
import concourse.bacc as bacc
import concourse.mybir as mybir
from concourse.expressions import smax
from concourse.bass_utils import run_bass_kernel_spmd

F32 = mybir.dt.float32
BF16 = mybir.dt.bfloat16
I32 = mybir.dt.int32
F16 = mybir.dt.float16
AF = mybir.ActivationFunctionType
ALU = mybir.AluOpType
AX = mybir.AxisListType

NC = 8
T = 1024
TS = 512
D = 2048
KC = 16
NH, NKV, HD = 32, 8, 64
NE = 16
NPASS = 20
IMOE = 1408
NKI = 11
LN_EPS = 1e-5

STAGE = 4
DEBUG = 0
TRACE = False



def _lhsT_tiles(W):
    K, M = W.shape
    Kc, Mc = K // 128, M // 128
    return np.ascontiguousarray(
        W.reshape(Kc, 128, Mc, 128).transpose(2, 1, 0, 3).reshape(Mc, 128, Kc * 128),
        dtype=np.float32)


def _rope_a_tables():
    inv = 1.0 / (10000.0 ** (np.arange(0, HD, 2, dtype=np.float64) / HD))
    pos = np.arange(NC * T, dtype=np.float64)
    fr = pos[:, None] * inv[None, :]
    return np.cos(fr).astype(np.float32), np.sin(fr).astype(np.float32)


def _rope_b_tables():
    d_half = HD // 2
    freq = 150000.0 ** (np.arange(0, HD, 2, dtype=np.float32) / HD)
    conc = 0.1 * math.log(32.0) + 1.0
    low = d_half * math.log(1024 / (32.0 * 2 * math.pi)) / math.log(150000.0)
    high = d_half * math.log(1024 / (1.0 * 2 * math.pi)) / math.log(150000.0)
    interp = 1.0 / (32.0 * freq)
    extrap = 1.0 / freq
    ramp = np.clip((np.arange(d_half, dtype=np.float32) - low) / (high - low), 0.0, 1.0)
    mask = 1.0 - ramp
    inv = interp * (1.0 - mask) + extrap * mask
    fr = np.outer(np.arange(NC * T, dtype=np.float32), inv)
    return (np.cos(fr) * conc).astype(np.float32), (np.sin(fr) * conc).astype(np.float32)


def _cs_tiles(c, s, t0):
    cs = c[t0:t0 + T].T
    ss = s[t0:t0 + T].T
    Cq = np.concatenate([cs, cs], 0)
    Sq = np.concatenate([-ss, ss], 0)
    return (np.concatenate([Cq, Cq], 0).astype(np.float16),
            np.concatenate([Sq, Sq], 0).astype(np.float16))


def _masks(core):
    i = np.arange(128)[:, None]
    j = np.arange(256)[None, :]
    mg = np.where((j > i) & (j <= i + 128), 0.0, -1e30).astype(np.float32)
    if core == 0:
        m0 = np.where((j >= 128) & (j - 128 <= i), 0.0, -1e30).astype(np.float32)
    else:
        m0 = mg
    return m0, mg


def _prepare(inputs):
    x = np.asarray(inputs["L0_attn_a_hidden_states"], np.float32).reshape(NC * T, D)

    shared = {}
    wq = _lhsT_tiles(np.asarray(inputs["L0_attn_a_Wq"], np.float32))
    wk = _lhsT_tiles(np.asarray(inputs["L0_attn_a_Wk"], np.float32))
    wv = _lhsT_tiles(np.asarray(inputs["L0_attn_a_Wv"], np.float32))
    shared["wqkva"] = np.ascontiguousarray(np.concatenate([wq, wk, wv], 0))
    shared["wphi"] = _lhsT_tiles(np.asarray(inputs["L0_attn_a_Wphi"], np.float32))
    shared["wo"] = _lhsT_tiles(np.asarray(inputs["L0_attn_a_Wo"], np.float32))
    shared["wqkvb"] = _lhsT_tiles(np.ascontiguousarray(
        np.asarray(inputs["L0_attn_b_qkv_w"], np.float32).T))
    shared["wout"] = _lhsT_tiles(np.ascontiguousarray(
        np.asarray(inputs["L0_attn_b_out_w"], np.float32).T))

    wg = np.asarray(inputs["L0_ffn_w_gate"], np.float32)
    wu = np.asarray(inputs["L0_ffn_w_up"], np.float32)
    wd = np.asarray(inputs["L0_ffn_w_down"], np.float32)
    shg = np.asarray(inputs["L0_ffn_sh_gate"], np.float32)
    shu = np.asarray(inputs["L0_ffn_sh_up"], np.float32)
    shd = np.asarray(inputs["L0_ffn_sh_down"], np.float32)
    gu = np.zeros((NPASS, NKI, 128, 4096), np.float32)
    dn = np.zeros((NPASS, KC, 128, IMOE), np.float32)
    for e in range(NE):
        gu[e, :, :, 0:2048] = _lhsT_tiles(wg[e])
        gu[e, :, :, 2048:4096] = _lhsT_tiles(wu[e])
        dn[e] = _lhsT_tiles(wd[e])
    for sidx in range(4):
        sl = slice(IMOE * sidx, IMOE * (sidx + 1))
        gu[NE + sidx, :, :, 0:2048] = _lhsT_tiles(shg[:, sl])
        gu[NE + sidx, :, :, 2048:4096] = _lhsT_tiles(shu[:, sl])
        dn[NE + sidx] = _lhsT_tiles(shd[sl, :])
    shared["wgu"] = np.ascontiguousarray(gu.reshape(NPASS * NKI * 128, 4096))
    shared["wdn"] = np.ascontiguousarray(
        dn.reshape(NPASS, KC // 2, 2, 128, IMOE).transpose(0, 1, 3, 2, 4)
        .reshape(NPASS * (KC // 2) * 128, 2 * IMOE))

    rw = np.asarray(inputs["L0_ffn_router_w"], np.float32)
    shared["router_t"] = np.ascontiguousarray(
        rw.T.reshape(KC, 128, NE).transpose(1, 0, 2).reshape(128, KC * NE))
    sgw = np.asarray(inputs["L0_ffn_sh_gate_w"], np.float32).reshape(KC, 128)
    shared["shgw_t"] = np.ascontiguousarray(sgw.T)
    shared["qkvb_t"] = np.ascontiguousarray(
        np.asarray(inputs["L0_attn_b_qkv_b"], np.float32).reshape(24, 128).T)
    shared["outb_t"] = np.ascontiguousarray(
        np.asarray(inputs["L0_attn_b_out_b"], np.float32).reshape(16, 128).T)
    shared["sinks_t"] = np.asarray(inputs["L0_attn_b_sinks"], np.float32).reshape(1, NH)
    shared["identb"] = np.eye(128, dtype=ml_dtypes.bfloat16)
    shared["identf"] = np.eye(128, dtype=np.float32)
    shared["ident16"] = np.eye(128, dtype=np.float16)

    ca, sa = _rope_a_tables()
    cb, sb = _rope_b_tables()
    per_core = []
    for c in range(NC):
        t0 = c * T
        Ca, Sa = _cs_tiles(ca, sa, t0)
        Cb, Sb = _cs_tiles(cb, sb, t0)
        m0, mg = _masks(c)
        xin = np.ascontiguousarray(x[t0:t0 + T].T.reshape(KC, 128, T))
        offs = np.array([[((c - 1) % NC) * T * 128, 0]], np.int32)
        d = dict(shared)
        d.update(xin=xin, cosa=Ca, sina=Sa, cosb=Cb, sinb=Sb,
                 mask0=m0, maskg=mg, offs=offs)
        per_core.append(d)
    return per_core



class Bld:
    def __init__(self, nc, es):
        self.nc = nc
        self.es = es
        self.sem = {}
        self.cnt = {}

    def S(self, name):
        if name not in self.sem:
            self.sem[name] = self.es.enter_context(self.nc.semaphore(name))
            self.cnt[name] = 0
        return self.sem[name]

    def c(self, name):
        self.S(name)
        return self.cnt[name]

    def inc(self, instr, name, n=1):
        instr.then_inc(self.S(name), n)
        self.cnt[name] += n

    def bump(self, name, n):
        self.S(name)
        self.cnt[name] += n

    def w(self, eng, name, val=None):
        eng.wait_ge(self.S(name), self.cnt[name] if val is None else val)

    def rg(self, eng, expr, fn):
        with eng.register() as r:
            eng.reg_mov(r, expr)
            return fn(r)

    pe_base = 0

    def g(self, t, base):
        import os
        if base != self.pe_base:
            if os.environ.get("KSIM"):
                t.nop()
            else:
                t.nop(cycle_cnt=256)
            self.pe_base = base


def _build(stage, debug):
    nc = bacc.Bacc("TRN2", target_bir_lowering=False)
    es = ExitStack()
    B = Bld(nc, es)

    xin = nc.dram_tensor("xin", [KC, 128, T], F32, kind="ExternalInput")
    wqkva = nc.dram_tensor("wqkva", [24, 128, 2048], F32, kind="ExternalInput")
    wphi = nc.dram_tensor("wphi", [16, 128, 2048], F32, kind="ExternalInput")
    wo = nc.dram_tensor("wo", [16, 128, 2048], F32, kind="ExternalInput")
    wqkvb = nc.dram_tensor("wqkvb", [24, 128, 2048], F32, kind="ExternalInput")
    wout = nc.dram_tensor("wout", [16, 128, 2048], F32, kind="ExternalInput")
    wgu = nc.dram_tensor("wgu", [NPASS * NKI * 128, 4096], F32, kind="ExternalInput")
    wdn = nc.dram_tensor("wdn", [NPASS * (KC // 2) * 128, 2 * IMOE], F32,
                         kind="ExternalInput")
    router_t = nc.dram_tensor("router_t", [128, KC * NE], F32, kind="ExternalInput")
    shgw_t = nc.dram_tensor("shgw_t", [128, KC], F32, kind="ExternalInput")
    qkvb_t = nc.dram_tensor("qkvb_t", [128, 24], F32, kind="ExternalInput")
    outb_t = nc.dram_tensor("outb_t", [128, 16], F32, kind="ExternalInput")
    sinks_t = nc.dram_tensor("sinks_t", [1, NH], F32, kind="ExternalInput")
    identb_d = nc.dram_tensor("identb", [128, 128], BF16, kind="ExternalInput")
    identf_d = nc.dram_tensor("identf", [128, 128], F32, kind="ExternalInput")
    cosa_d = nc.dram_tensor("cosa", [128, T], F16, kind="ExternalInput")
    sina_d = nc.dram_tensor("sina", [128, T], F16, kind="ExternalInput")
    cosb_d = nc.dram_tensor("cosb", [128, T], F16, kind="ExternalInput")
    sinb_d = nc.dram_tensor("sinb", [128, T], F16, kind="ExternalInput")
    ident16_d = nc.dram_tensor("ident16", [128, 128], F16, kind="ExternalInput")
    mask0_d = nc.dram_tensor("mask0", [128, 256], F32, kind="ExternalInput")
    maskg_d = nc.dram_tensor("maskg", [128, 256], F32, kind="ExternalInput")
    offs_d = nc.dram_tensor("offs", [1, 2], I32, kind="ExternalInput")
    OUT = nc.dram_tensor("out", [T, D], F32, kind="ExternalOutput")
    if debug:
        DBG = nc.dram_tensor("dbg", [KC, 128, T], F32, kind="ExternalOutput")

    cc1_in = nc.dram_tensor("cc1_in", [128, 16], F32)
    cc1_out = nc.dram_tensor("cc1_out", [128, 16], F32)
    cc2_in = nc.dram_tensor("cc2_in", [1, 256], F32)
    cc2_out = nc.dram_tensor("cc2_out", [1, 256], F32)
    cc3_in = nc.dram_tensor("cc3_in", [64, 2048], F16)
    cc3_out = nc.dram_tensor("cc3_out", [64, 2048], F16)
    cc3d_in = nc.dram_tensor("cc3d_in", [1, 256], F32)
    cc3d_out = nc.dram_tensor("cc3d_out", [1, 256], F32)
    combT_d = nc.dram_tensor("combT_d", [NPASS, T], F16)
    kv_in = nc.dram_tensor("kv_in", [T, 128], F16)
    kv_out = nc.dram_tensor("kv_out", [NC * T, 128], F16)

    sbt = lambda *a, **k: es.enter_context(nc.sbuf_tensor(*a, **k))
    x_all = sbt("x_all", [128, KC * T], F32)
    hn_all = sbt("hn_all", [128, KC * T], F16)
    cosa = sbt("cosa_s", [128, T], F16)
    sina = sbt("sina_s", [128, T], F16)
    ident16 = sbt("ident16_s", [128, 128], F16)
    identb = sbt("identb_s", [128, 128], BF16)
    identf = sbt("identf_s", [128, 128], F32)
    qkvb_s = sbt("qkvb_s", [128, 24], F32)
    outb_s = sbt("outb_s", [128, 16], F32)
    sinkbc = sbt("sinkbc", [128, NH], F32)
    router_s = sbt("router_s", [128, KC * NE], F16)
    shgw_s = sbt("shgw_s", [128, KC], F16)
    ones_r = sbt("ones_r", [1, 128], F16)
    ones_c = sbt("ones_c", [128, 1], F16)
    sinks_bf = sbt("sinks_bf", [1, NH], F16)
    offs_s = sbt("offs_s", [1, 2], I32)
    eps_c = sbt("eps_c", [1, 1], F32)

    import os
    if os.environ.get("KDBG"):
        print("[sbuf] after persistent:", nc.sbuf_bytes_remaining)
    blk = es.enter_context(nc.Block())

    def xc(k, lo=0, hi=T):
        return x_all[:, k * T + lo:k * T + hi]

    def hc(k, lo=0, hi=T):
        return hn_all[:, k * T + lo:k * T + hi]

    def ph_setup():
        @blk.sync
        def _(sy):
            for k in range(KC):
                B.inc(sy.dma_start(out=xc(k), in_=xin[k, :, :]), "s_x", 16)
            for dst, src in ((identb, identb_d), (identf, identf_d),
                             (ident16, ident16_d),
                             (qkvb_s, qkvb_t), (outb_s, outb_t),
                             (cosa, cosa_d), (sina, sina_d),
                             (offs_s, offs_d)):
                B.inc(sy.dma_start(out=dst[:, :], in_=src[:, :]), "s_cn", 16)

        @blk.gpsimd
        def _(g):
            B.inc(g.dma_start(out=router_s[:, :], in_=router_t[:, :]), "s_cn2", 16)
            B.inc(g.dma_start(out=shgw_s[:, :], in_=shgw_t[:, :]), "s_cn2", 16)
            B.inc(g.dma_start(out=sinks_bf[:, :], in_=sinks_t[:, :]), "s_cn2", 16)

        @blk.vector
        def _(v):
            B.inc(v.memset(ones_r[:, :], 1.0), "s_ones")
            B.inc(v.memset(ones_c[:, :], 1.0), "s_ones")
            B.inc(v.memset(eps_c[:, :], LN_EPS), "s_ones")

        with nc.psum_tensor("p_snk", [128, NH], F32) as p_snk:
            @blk.tensor
            def _(t):
                B.w(t, "s_cn")
                B.w(t, "s_cn2")
                B.w(t, "s_ones")
                B.inc(t.matmul(p_snk[:, :], ones_r[:, :], sinks_bf[:, :],
                               start=True, stop=True), "s_pe0")

            @blk.scalar
            def _(s):
                B.w(s, "s_pe0")
                B.inc(s.copy(sinkbc[:, :], p_snk[:, :]), "s_snk")

    def ph_ln(tag, pre, sb=1.0, sq=16.0, sr=1.0):
        with nc.sbuf_tensor(f"xb_{tag}", [128, 2 * T], F16) as xb, \
             nc.sbuf_tensor(f"xq_{tag}", [128, 2 * T], F16) as xq, \
             nc.sbuf_tensor(f"rw_{tag}", [1, 3 * T], F32) as rows, \
             nc.sbuf_tensor(f"rb_{tag}", [1, 2 * T], F16) as rows_bf, \
             nc.sbuf_tensor(f"mb_{tag}", [128, T], F16) as mbc, \
             nc.sbuf_tensor(f"rc_{tag}", [128, T], F16) as rbc, \
             nc.sbuf_tensor(f"dd_{tag}", [128, 2 * T], F32) as dsc, \
             nc.psum_tensor(f"psm_{tag}", [1, T], F32) as p_sum, \
             nc.psum_tensor(f"psq_{tag}", [1, T], F32) as p_sq, \
             nc.psum_tensor(f"pb0_{tag}", [128, T], F32) as p_mb, \
             nc.psum_tensor(f"pb1_{tag}", [128, T], F32) as p_rb:

            t_sc = f"{tag}_sc"; t_pe = f"{tag}_pe"; t_rw = f"{tag}_rw"
            t_s2 = f"{tag}_s2"; t_r2 = f"{tag}_r2"; t_s3 = f"{tag}_s3"
            t_rf = f"{tag}_rf"; t_bc = f"{tag}_bc"; t_ev = f"{tag}_ev"
            t_d = f"{tag}_d"; t_fr = f"{tag}_fr"; t_nm = f"{tag}_nm"

            base_fr = B.c(t_fr)

            @blk.scalar
            def _(s):
                if pre is None:
                    B.w(s, "s_x")
                else:
                    s.wait_ge(B.S(pre[0]), pre[1])
                for k in range(KC):
                    par = k % 2
                    if k >= 2:
                        s.wait_ge(B.S(t_fr), base_fr + k - 1)
                    if sb == 1.0:
                        B.inc(s.copy(xb[:, par * T:(par + 1) * T], xc(k)), t_sc)
                    else:
                        B.inc(s.activation(xb[:, par * T:(par + 1) * T], xc(k),
                                           AF.Copy, scale=1.0 / sb), t_sc)
                    B.inc(s.activation(xq[:, par * T:(par + 1) * T], xc(k), AF.Square,
                                       scale=1.0 / sq), t_sc)

            base_sc = B.c(t_sc) - 2 * KC

            @blk.tensor
            def _(t):
                last = None
                for k in range(KC):
                    par = k % 2
                    t.wait_ge(B.S(t_sc), base_sc + 2 * (k + 1))
                    for ts_i in range(2):
                        sl = slice(ts_i * TS, (ts_i + 1) * TS)
                        xsl = slice(par * T + ts_i * TS, par * T + (ts_i + 1) * TS)
                        t.matmul(p_sum[:, sl], ones_c[:, :], xb[:, xsl],
                                 start=(k == 0), stop=(k == KC - 1))
                        last = t.matmul(p_sq[:, sl], ones_c[:, :], xq[:, xsl],
                                        start=(k == 0), stop=(k == KC - 1))
                    if k < KC - 1:
                        B.inc(last, t_fr)
                B.inc(last, t_pe)

            @blk.vector
            def _(v):
                B.w(v, t_pe)
                B.inc(v.tensor_scalar_mul(rows[:, 0:T], p_sum[:, :], sb / D), t_rw)
                B.inc(v.tensor_scalar_mul(rows[:, T:2 * T], p_sq[:, :],
                                          sq * sq / D), t_rw)

            @blk.scalar
            def _(s):
                B.w(s, t_rw)
                B.inc(s.activation(rows[:, 2 * T:3 * T], rows[:, 0:T], AF.Square), t_s2)

            @blk.vector
            def _(v):
                B.w(v, t_s2)
                B.inc(v.tensor_sub(rows[:, T:2 * T], rows[:, T:2 * T],
                                   rows[:, 2 * T:3 * T]), t_r2)

            @blk.scalar
            def _(s):
                B.w(s, t_r2)
                B.w(s, "s_ones")
                B.inc(s.activation(rows[:, 2 * T:3 * T], rows[:, T:2 * T], AF.Sqrt,
                                   bias=eps_c[0:1, 0:1]), t_s3)

            @blk.vector
            def _(v):
                B.w(v, t_s3)
                B.inc(v.reciprocal(rows[:, T:2 * T], rows[:, 2 * T:3 * T]), t_rf)

            @blk.scalar
            def _(s):
                B.w(s, t_rf)
                B.inc(s.copy(rows_bf[:, 0:T], rows[:, 0:T]), t_s3)
                if sr == 1.0:
                    B.inc(s.copy(rows_bf[:, T:2 * T], rows[:, T:2 * T]), t_s3)
                else:
                    B.inc(s.activation(rows_bf[:, T:2 * T], rows[:, T:2 * T],
                                       AF.Copy, scale=sr), t_s3)

            @blk.tensor
            def _(t):
                B.w(t, t_s3)
                for ts_i in range(2):
                    sl = slice(ts_i * TS, (ts_i + 1) * TS)
                    t.matmul(p_mb[:, sl], ones_r[:, :], rows_bf[0:1, sl],
                             start=True, stop=True)
                    B.inc(t.matmul(p_rb[:, sl], ones_r[:, :],
                                   rows_bf[0:1, T + ts_i * TS:T + (ts_i + 1) * TS],
                                   start=True, stop=True), t_bc)

            @blk.scalar
            def _(s):
                B.w(s, t_bc)
                B.inc(s.copy(mbc[:, :], p_mb[:, :]), t_ev)
                B.inc(s.copy(rbc[:, :], p_rb[:, :]), t_ev)

            @blk.vector
            def _(v):
                B.w(v, t_ev)
                for k in range(KC):
                    par = k % 2
                    dsl = slice(par * T, (par + 1) * T)
                    B.inc(v.tensor_sub(dsc[:, dsl], xc(k), mbc[:, :]), t_d)
                    B.w(v, t_d)
                    if sr == 1.0:
                        B.inc(v.tensor_mul(hc(k), dsc[:, dsl], rbc[:, :]), t_nm)
                    else:
                        B.inc(v.scalar_tensor_tensor(hc(k), dsc[:, dsl], 1.0 / sr,
                                                     rbc[:, :], op0=ALU.mult,
                                                     op1=ALU.mult), t_nm)
        return t_nm

    def emit_proj(tagp, wdram, n_chunks, dst_t, dst_stride, gate, bias_s=None):
        t_ld = (f"{tagp}_ld0", f"{tagp}_ld1")
        t_pe = f"{tagp}_pe"; t_ev = f"{tagp}_ev"
        with nc.psum_tensor(f"pp0_{tagp}", [128, T], F32) as pp0, \
             nc.psum_tensor(f"pp1_{tagp}", [128, T], F32) as pp1, \
             nc.sbuf_tensor(f"ws_{tagp}", [128, 2 * 2048], F16) as wstream:
            pp = (pp0, pp1)

            @blk.gpsimd
            def _(g):
                with g.register(f"go_{tagp}") as rof:
                    g.reg_mov(rof, 0)
                    with g.Fori(0, n_chunks // 2) as i:
                        for par in range(2):
                            m = i * 2 + par
                            g.wait_ge(B.S(t_pe), smax(m - 1, 0))
                            g.dma_start(
                                out=wstream[:, par * 2048:(par + 1) * 2048],
                                in_=bass.AP(wdram, rof, [[2048, 128], [1, 2048]])
                                ).then_inc(B.S(t_ld[par]), 16)
                            g.reg_alu(rof, rof, 128 * 2048, ALU.add)
            B.bump(t_ld[0], 16 * (n_chunks // 2))
            B.bump(t_ld[1], 16 * (n_chunks // 2))

            @blk.tensor
            def _(t):
                t.wait_ge(B.S(gate[0]), gate[1])
                with t.Fori(0, n_chunks // 2) as i:
                    for par in range(2):
                        m = i * 2 + par
                        t.wait_ge(B.S(t_ld[par]), (i + 1) * 16)
                        t.wait_ge(B.S(t_ev), smax(m - 1, 0))
                        last = None
                        for k in range(KC):
                            lh = wstream[:, par * 2048 + k * 128:par * 2048 + (k + 1) * 128]
                            for ts_i in range(2):
                                last = t.matmul(
                                    pp[par][:, ts_i * TS:(ts_i + 1) * TS], lh,
                                    hc(k, ts_i * TS, (ts_i + 1) * TS),
                                    start=(k == 0), stop=(k == KC - 1))
                        last.then_inc(B.S(t_pe), 1)
            B.bump(t_pe, n_chunks)

            @blk.scalar
            def _(s):
                with s.register(f"so_{tagp}") as rof:
                    s.reg_mov(rof, 0)
                    with s.Fori(0, n_chunks // 2) as i:
                        for par in range(2):
                            m = i * 2 + par
                            s.wait_ge(B.S(t_pe), m + 1)
                            s.activation(
                                bass.AP(dst_t, rof, [[dst_stride, 128], [1, T]]),
                                pp[par][:, :], AF.Copy).then_inc(B.S(t_ev), 1)
                            s.reg_alu(rof, rof, T, ALU.add)
            B.bump(t_ev, n_chunks)
        return t_ev

    def emit_rope(tagr, qa, chunks, Ctab, Stab, kappa, gate, extra16=None):
        t_s = f"{tagr}_s"; t_v = f"{tagr}_v"
        with nc.sbuf_tensor(f"rot_{tagr}", [128, 2 * T], F16) as rot, \
             nc.sbuf_tensor(f"sc1_{tagr}", [128, 2 * T], F16) as sc1, \
             nc.sbuf_tensor(f"sc2_{tagr}", [128, 2 * T], F16) as sc2:
            base_s = B.c(t_s)
            base_v = B.c(t_v)
            spc = 2 if kappa else 1
            vpc = 6 if kappa else 4
            for j, ch in enumerate(chunks):
                par = j % 2
                qs = lambda r0, r1: qa[r0:r1, ch * T:(ch + 1) * T]
                qf = qa[:, ch * T:(ch + 1) * T]
                rsl = lambda r0, r1: rot[r0:r1, par * T:(par + 1) * T]
                rfl = rot[:, par * T:(par + 1) * T]
                s1 = sc1[:, par * T:(par + 1) * T]
                s2 = sc2[:, par * T:(par + 1) * T]

                @blk.scalar
                def _(s, j=j, ch=ch, par=par, qs=qs, rsl=rsl):
                    if gate is not None:
                        s.wait_ge(B.S(gate[0]), gate[1](ch))
                    if j >= 2:
                        s.wait_ge(B.S(t_v), base_v + (j - 1) * vpc)
                    s.copy(rsl(0, 32), qs(32, 64))
                    s.copy(rsl(32, 64), qs(0, 32))
                    s.copy(rsl(64, 96), qs(96, 128))
                    s.copy(rsl(96, 128), qs(64, 96)).then_inc(B.S(t_s), 1)

                @blk.vector
                def _(v, j=j, par=par, qf=qf, rfl=rfl, s1=s1, s2=s2):
                    v.wait_ge(B.S(t_s), base_s + j * spc + 1)
                    v.tensor_mul(s1, qf, Ctab[:, :]).then_inc(B.S(t_v), 1)
                    v.tensor_mul(s2, rfl, Stab[:, :]).then_inc(B.S(t_v), 1)
                    v.wait_ge(B.S(t_v), base_v + j * vpc + 2)
                    v.tensor_add(s1, s1, s2).then_inc(B.S(t_v), 1)

                if kappa:
                    @blk.vector
                    def _(v, j=j, s1=s1, s2=s2):
                        v.wait_ge(B.S(t_v), base_v + j * vpc + 3)
                        v.tensor_scalar_min(s2, s1, 0.0).then_inc(B.S(t_v), 1)
                        v.wait_ge(B.S(t_v), base_v + j * vpc + 4)
                        v.tensor_scalar_max(s1, s1, 0.0).then_inc(B.S(t_v), 1)

                    @blk.scalar
                    def _(s, j=j, s2=s2):
                        s.wait_ge(B.S(t_v), base_v + j * vpc + 4)
                        s.activation(s2, s2, AF.Exp).then_inc(B.S(t_s), 1)

                    @blk.vector
                    def _(v, j=j, ch=ch, qf=qf, s1=s1, s2=s2):
                        v.wait_ge(B.S(t_s), base_s + j * spc + 2)
                        v.wait_ge(B.S(t_v), base_v + j * vpc + 5)
                        if extra16 is not None and ch >= extra16[1]:
                            e16 = extra16[0]
                            cc = ch - extra16[1]
                            v.tensor_add(e16[:, cc * T:(cc + 1) * T], s1, s2)
                        v.tensor_add(qf, s1, s2).then_inc(B.S(t_v), 1)
                else:
                    @blk.vector
                    def _(v, j=j, qf=qf, s1=s1):
                        v.wait_ge(B.S(t_v), base_v + j * vpc + 3)
                        v.tensor_copy(qf, s1).then_inc(B.S(t_v), 1)
            B.bump(t_s, spc * len(chunks))
            B.bump(t_v, vpc * len(chunks))
        return t_v

    def ph_attn_a(ln_gate):
        import os
        KSUB = int(os.environ.get("KSUB", "9"))
        es2 = ExitStack()
        qa = es2.enter_context(nc.sbuf_tensor("qa1", [128, 24 * T], F16))
        kk_tm = es2.enter_context(nc.sbuf_tensor("kk_tm", [128, 8 * 512], F16))
        v_tm = es2.enter_context(nc.sbuf_tensor("v_tm", [128, 8 * 512], F16))
        qgp = es2.enter_context(nc.sbuf_tensor("qgp", [128, 16], F32))
        qgT = es2.enter_context(nc.sbuf_tensor("qgT", [128, 32], F16))
        qgT32 = es2.enter_context(nc.sbuf_tensor("qgT32", [64, 32], F32))
        kk16 = es2.enter_context(nc.sbuf_tensor("kk16", [128, 4 * T], F16))
        e_sb = es2.enter_context(nc.sbuf_tensor("e_sb", [64, 4096], F16))
        e_tm = es2.enter_context(nc.sbuf_tensor("e_tm", [128, 2048], F16))
        lmx = es2.enter_context(nc.sbuf_tensor("lmx", [64, 4], F32))
        nmx = es2.enter_context(nc.sbuf_tensor("nmx", [64, 4], F32))
        dloc = es2.enter_context(nc.sbuf_tensor("dloc", [64, 4], F32))
        U_sb = es2.enter_context(nc.sbuf_tensor("U_sb", [128, 2048], F16))
        ekkb = es2.enter_context(nc.sbuf_tensor("ekk", [128, 128], F16))
        rec32 = es2.enter_context(nc.sbuf_tensor("rec32", [1, 256], F32))
        rcrow = es2.enter_context(nc.sbuf_tensor("rcrow", [1, 256], F32))
        rcbf = es2.enter_context(nc.sbuf_tensor("rcbf", [1, 256], F16))
        r_col = es2.enter_context(nc.sbuf_tensor("r_col", [128, 16], F32))
        if True:
            import os
            if os.environ.get("KDBG"):
                print("[sbuf] after attn_a es2:", nc.sbuf_bytes_remaining)
            ev_p = emit_proj("pqa", wqkva, 24, qa, 24 * T, (ln_gate, B.c(ln_gate)))
            pev_total = B.c(ev_p)

            if KSUB < 2:
                es2.close()
                return None
            gate_fn = lambda ch: pev_total - 24 + ch + 1
            rq = emit_rope("rpa", qa, list(range(20)), cosa, sina, True,
                           (ev_p, gate_fn), extra16=(kk16, 16))
            rk = rq
            rq_total = B.c(rq)
            rk_total = B.c(rk)

            if KSUB < 3:
                es2.close()
                return None
            @blk.vector
            def _(v):
                v.wait_ge(B.S(rq), rq_total - 4 * 5)
                for k in range(KC):
                    B.inc(v.reduce_sum(qgp[:, k:k + 1], qa[:, k * T:(k + 1) * T],
                                       axis=AX.X), "a_qg")

            @blk.sync
            def _(sy):
                B.w(sy, "a_qg")
                B.inc(sy.dma_start(cc1_in[:, :], qgp[:, :]), "a_st1", 16)

            @blk.gpsimd
            def _(g):
                B.w(g, "a_st1")
                B.inc(g.collective_compute(
                    "AllReduce", ALU.add, replica_groups=[list(range(NC))],
                    ins=[cc1_in.ap()], outs=[cc1_out.ap()]), "a_cc1")
                B.w(g, "a_cc1")
                for h in range(2):
                    B.inc(g.dma_start(out=qgT32[0:64, h * 16:(h + 1) * 16],
                                      in_=cc1_out[h * 64:(h + 1) * 64, :]),
                          "a_qgT", 16)

            if KSUB < 4:
                es2.close()
                return None
            @blk.scalar
            def _(s):
                B.w(s, "a_qgT")
                B.inc(s.activation(qgT[0:64, :], qgT32[0:64, :], AF.Copy,
                                   scale=1.0 / 8192.0), "a_qg16")
                B.inc(s.activation(qgT[64:128, :], qgT32[0:64, :], AF.Copy,
                                   scale=1.0 / 8192.0), "a_qg16")

            esp1 = ExitStack()
            pkt0 = esp1.enter_context(nc.psum_tensor("pkt0", [128, 512], F16))
            pkt1 = esp1.enter_context(nc.psum_tensor("pkt1", [128, 512], F16))
            pvt0 = esp1.enter_context(nc.psum_tensor("pvt0", [128, 512], F16))
            pvt1 = esp1.enter_context(nc.psum_tensor("pvt1", [128, 512], F16))
            if True:
                pkt = (pkt0, pkt1)
                pvt = (pvt0, pvt1)

                @blk.tensor
                def _(t):
                    t.wait_ge(B.S(rk), rk_total)
                    t.wait_ge(B.S(ev_p), pev_total)
                    base_ev = B.c("a_tev")
                    for ti in range(8):
                        par = ti % 2
                        if ti >= 2:
                            t.wait_ge(B.S("a_tev"), base_ev + 2 * (ti - 1))
                        last = None
                        B.g(t, 0)
                        for cc in range(4):
                            srcp = kk16[:, cc * T + ti * 128:cc * T + (ti + 1) * 128]
                            last = t.transpose(pkt[par][:, cc * 128:(cc + 1) * 128],
                                               srcp, ident16[:, :])
                        B.inc(last, "a_tpe")
                        for cc in range(4):
                            srcp = qa[:, (20 + cc) * T + ti * 128:
                                      (20 + cc) * T + (ti + 1) * 128]
                            last = t.transpose(pvt[par][:, cc * 128:(cc + 1) * 128],
                                               srcp, ident16[:, :])
                        B.inc(last, "a_tpe")

                @blk.scalar
                def _(s):
                    base = B.c("a_tpe") - 16
                    for ti in range(8):
                        par = ti % 2
                        s.wait_ge(B.S("a_tpe"), base + 2 * ti + 1)
                        B.inc(s.copy(kk_tm[:, ti * 512:(ti + 1) * 512],
                                     pkt[par][:, :]), "a_tev")
                        s.wait_ge(B.S("a_tpe"), base + 2 * ti + 2)
                        B.inc(s.copy(v_tm[:, ti * 512:(ti + 1) * 512],
                                     pvt[par][:, :]), "a_tev")

            if KSUB < 5:
                es2.close()
                return None
            esp1.close()
            esp2 = ExitStack()
            plq0 = esp2.enter_context(nc.psum_tensor("plq0", [64, T], F32))
            plq1 = esp2.enter_context(nc.psum_tensor("plq1", [64, T], F32))
            plq2 = esp2.enter_context(nc.psum_tensor("plq2", [64, T], F32))
            plq3 = esp2.enter_context(nc.psum_tensor("plq3", [64, T], F32))
            if True:
                plq = (plq0, plq1, plq2, plq3)

                @blk.tensor
                def _(t):
                    B.w(t, "a_qg16")
                    B.w(t, "a_tev")
                    last = None
                    for par2 in range(2):
                        b2 = par2 * 64
                        B.g(t, b2)
                        for g8 in range(par2, 8, 2):
                            q4 = g8 // 2
                            r0 = 32 * (g8 % 2)
                            for ts_i in range(2):
                                srcp = kk16[b2:b2 + 64,
                                            (g8 // 2) * T + ts_i * TS:
                                            (g8 // 2) * T + (ts_i + 1) * TS]
                                last = t.matmul(
                                    plq[q4][r0:r0 + 32, ts_i * TS:(ts_i + 1) * TS],
                                    qgT[b2:b2 + 64, 0:32], srcp,
                                    start=True, stop=True)
                    B.inc(last, "a_lg")

                @blk.vector
                def _(v):
                    B.w(v, "a_lg")
                    for q4 in range(4):
                        B.inc(v.reduce_max(lmx[:, q4:q4 + 1], plq[q4][:, :],
                                           axis=AX.X), "a_lm")

                @blk.sync
                def _(sy):
                    B.w(sy, "a_lm")
                    for q4 in range(4):
                        B.inc(sy.dma_start(cc2_in[0:1, q4 * 64:(q4 + 1) * 64],
                                           lmx[:, q4:q4 + 1]), "a_st2", 16)

                @blk.gpsimd
                def _(g):
                    B.w(g, "a_st2")
                    B.inc(g.collective_compute(
                        "AllReduce", ALU.max, replica_groups=[list(range(NC))],
                        ins=[cc2_in.ap()], outs=[cc2_out.ap()]), "a_cc2")

                @blk.sync
                def _(sy):
                    B.w(sy, "a_cc2")
                    for q4 in range(4):
                        B.inc(sy.dma_start(lmx[:, q4:q4 + 1],
                                           cc2_out[0:1, q4 * 64:(q4 + 1) * 64]),
                              "a_mx", 16)

                @blk.vector
                def _(v):
                    B.w(v, "a_mx")
                    B.inc(v.tensor_scalar(nmx[:, :], lmx[:, :], -1.0,
                                          -5.545177444479562, op0=ALU.mult,
                                          op1=ALU.add), "a_nm")

                @blk.scalar
                def _(s):
                    B.w(s, "a_nm")
                    for q4 in range(4):
                        B.inc(s.activation(e_sb[0:64, q4 * T:(q4 + 1) * T],
                                           plq[q4][:, :], AF.Exp,
                                           bias=nmx[:, q4:q4 + 1]), "a_e")

                @blk.vector
                def _(v):
                    B.w(v, "a_e")
                    for q4 in range(4):
                        B.inc(v.reduce_sum(dloc[:, q4:q4 + 1],
                                           e_sb[0:64, q4 * T:(q4 + 1) * T],
                                           axis=AX.X), "a_dl")

            if KSUB < 5:
                es2.close()
                return None
            esp2.close()
            esp3 = ExitStack()
            pet0 = esp3.enter_context(nc.psum_tensor("pet0", [128, 64], F16))
            pet1 = esp3.enter_context(nc.psum_tensor("pet1", [128, 64], F16))
            if True:
                pet = (pet0, pet1)

                @blk.tensor
                def _(t):
                    B.w(t, "a_e")
                    B.g(t, 0)
                    base_ev = B.c("a_eev")
                    idx = 0
                    for q4 in range(4):
                        for ti in range(8):
                            par = idx % 2
                            if idx >= 2:
                                t.wait_ge(B.S("a_eev"), base_ev + idx - 1)
                            B.inc(t.transpose(
                                pet[par][:, :],
                                e_sb[0:64, q4 * T + ti * 128:q4 * T + (ti + 1) * 128],
                                ident16[0:64, 0:64]), "a_ete")
                            idx += 1

                @blk.scalar
                def _(s):
                    base = B.c("a_ete") - 32
                    idx = 0
                    for q4 in range(4):
                        for ti in range(8):
                            par = idx % 2
                            s.wait_ge(B.S("a_ete"), base + idx + 1)
                            B.inc(s.copy(
                                e_tm[:, ti * 256 + q4 * 64:ti * 256 + (q4 + 1) * 64],
                                pet[par][:, :]), "a_eev")
                            idx += 1

            def slotrow(n):
                g4 = n // 4
                colj = (n % 2) * 16 + n // 2
                return g4 // 2, 32 * (g4 % 2) + colj

            if KSUB < 6:
                es2.close()
                return None
            esp3.close()
            esp4 = ExitStack()
            pU0 = esp4.enter_context(nc.psum_tensor("pU0", [64, 512], F32))
            pU1 = esp4.enter_context(nc.psum_tensor("pU1", [64, 512], F32))
            if True:
                pU = (pU0, pU1)

                @blk.vector
                def _(v):
                    B.w(v, "a_eev")
                    base_fr = B.c("a_ufr")
                    idx = 0
                    for b in range(4):
                        for n in range(8 * b, 8 * b + 8):
                            g = n // 4
                            q4, row = slotrow(n)
                            for ti in range(8):
                                par = idx % 2
                                if idx >= 2:
                                    v.wait_ge(B.S("a_ufr"), base_fr + idx - 1)
                                gcol = (g // 2) * 128 + (g % 2) * 64
                                B.inc(v.tensor_mul(
                                    ekkb[:, par * 64:(par + 1) * 64],
                                    kk_tm[:, ti * 512 + gcol:ti * 512 + gcol + 64],
                                    e_tm[:, ti * 256 + q4 * 64 + row:
                                         ti * 256 + q4 * 64 + row + 1
                                         ].to_broadcast([128, 64])), "a_ek")
                                idx += 1

                @blk.tensor
                def _(t):
                    B.w(t, "a_eev")
                    B.g(t, 0)
                    base_ek = B.c("a_ek") - 256
                    base_uev = B.c("a_uev")
                    idx = 0
                    for b in range(4):
                        if b >= 2:
                            t.wait_ge(B.S("a_uev"), base_uev + b - 1)
                        last = None
                        for n in range(8 * b, 8 * b + 8):
                            g = n // 4
                            for ti in range(8):
                                par = idx % 2
                                t.wait_ge(B.S("a_ek"), base_ek + idx + 1)
                                gcol = (g // 2) * 128 + (g % 2) * 64
                                last = t.matmul(
                                    pU[b % 2][0:64, (n - 8 * b) * 64:(n - 8 * b + 1) * 64],
                                    ekkb[:, par * 64:(par + 1) * 64],
                                    v_tm[:, ti * 512 + gcol:ti * 512 + gcol + 64],
                                    start=(ti == 0), stop=(ti == 7))
                                last.then_inc(B.S("a_ufr"), 1)
                                B.bump("a_ufr", 1)
                                idx += 1

                @blk.scalar
                def _(s):
                    base = B.c("a_ufr") - 256
                    for b in range(4):
                        s.wait_ge(B.S("a_ufr"), base + 64 * (b + 1))
                        B.inc(s.copy(U_sb[0:64, b * 512:(b + 1) * 512],
                                     pU[b % 2][:, :]), "a_uev")

            @blk.sync
            def _(sy):
                B.w(sy, "a_uev")
                B.inc(sy.dma_start(cc3_in[0:64, :], U_sb[0:64, :]), "a_st3", 16)
                B.w(sy, "a_dl")
                for q4 in range(4):
                    B.inc(sy.dma_start(cc3d_in[0:1, q4 * 64:(q4 + 1) * 64],
                                       dloc[:, q4:q4 + 1]), "a_st3", 16)

            @blk.gpsimd
            def _(g):
                B.w(g, "a_st3")
                B.inc(g.collective_compute(
                    "AllReduce", ALU.add, replica_groups=[list(range(NC))],
                    ins=[cc3_in.ap()], outs=[cc3_out.ap()]), "a_cc3")
                B.inc(g.collective_compute(
                    "AllReduce", ALU.add, replica_groups=[list(range(NC))],
                    ins=[cc3d_in.ap()], outs=[cc3d_out.ap()]), "a_cc3")

            @blk.sync
            def _(sy):
                B.w(sy, "a_cc3")
                B.inc(sy.dma_start(U_sb[0:64, :], cc3_out[0:64, :]), "a_bk", 16)
                B.inc(sy.dma_start(U_sb[64:128, :], cc3_out[0:64, :]), "a_bk", 16)
                B.inc(sy.dma_start(rec32[:, :], cc3d_out[0:1, 0:256]), "a_bk", 16)

            if KSUB < 7:
                es2.close()
                return None
            @blk.vector
            def _(v):
                B.w(v, "a_bk")
                B.inc(v.reciprocal(rec32[:, :], rec32[:, :]), "a_rc")
                B.w(v, "a_rc")
                B.inc(v.tensor_scalar_mul(rec32[:, :], rec32[:, :], 8.0), "a_rc")

            if KSUB < 7:
                es2.close()
                return None
            esp4.close()
            esp5 = ExitStack()
            prc0 = esp5.enter_context(nc.psum_tensor("prc0", [128, 1], F32))
            prc1 = esp5.enter_context(nc.psum_tensor("prc1", [128, 1], F32))
            if True:
                prc = (prc0, prc1)

                def dcol(n):
                    g = n // 4
                    colj = (n % 2) * 16 + n // 2
                    return (g // 2) * 64 + 32 * (g % 2) + colj

                rp_base = B.c("a_rp")
                rr_base = B.c("a_rr")
                rb_base = B.c("a_rb")
                rv_base = B.c("a_rv")

                @blk.vector
                def _(v):
                    B.w(v, "a_rc")
                    for k in range(KC):
                        par = k % 2
                        if k >= 2:
                            v.wait_ge(B.S("a_rb"), rb_base + k - 1)
                        c0, c1 = dcol(2 * k), dcol(2 * k + 1)
                        v.tensor_copy(rcrow[0:1, par * 128:par * 128 + 64],
                                      rec32[0:1, c0:c0 + 1].to_broadcast([1, 64]))
                        B.inc(v.tensor_copy(
                            rcrow[0:1, par * 128 + 64:par * 128 + 128],
                            rec32[0:1, c1:c1 + 1].to_broadcast([1, 64])),
                            "a_rr")

                @blk.scalar
                def _(s):
                    for k in range(KC):
                        par = k % 2
                        if k >= 2:
                            s.wait_ge(B.S("a_rp"), rp_base + k - 1)
                            B.inc(s.copy(r_col[:, k - 2:k - 1], prc[par][:, :]), "a_rv")
                        s.wait_ge(B.S("a_rr"), rr_base + k + 1)
                        B.inc(s.copy(rcbf[0:1, par * 128:(par + 1) * 128],
                                     rcrow[0:1, par * 128:(par + 1) * 128]), "a_rb")
                    for k in (KC - 2, KC - 1):
                        par = k % 2
                        s.wait_ge(B.S("a_rp"), rp_base + k + 1)
                        B.inc(s.copy(r_col[:, k:k + 1], prc[par][:, :]), "a_rv")

                @blk.tensor
                def _(t):
                    for k in range(KC):
                        par = k % 2
                        t.wait_ge(B.S("a_rb"), rb_base + k + 1)
                        if k >= 2:
                            t.wait_ge(B.S("a_rv"), rv_base + k - 1)
                        B.inc(t.matmul(prc[par][:, :],
                                       rcbf[0:1, par * 128:(par + 1) * 128],
                                       ones_r[0:1, 0:1], start=True, stop=True),
                              "a_rp")

            esp5.close()
            esp6 = ExitStack()
            esp6s = ExitStack()
            wstream = esp6s.enter_context(nc.sbuf_tensor("ws_aphi", [128, 2 * 2048], F16))
            phi_scr = esp6s.enter_context(nc.sbuf_tensor("phi_scr", [128, 2 * T], F16))
            pph0 = esp6.enter_context(nc.psum_tensor("pph0", [128, T], F32))
            pph1 = esp6.enter_context(nc.psum_tensor("pph1", [128, T], F32))
            prs0 = esp6.enter_context(nc.psum_tensor("prs0", [128, T], F32))
            prs1 = esp6.enter_context(nc.psum_tensor("prs1", [128, T], F32))
            if True:
                pph = (pph0, pph1)
                prs = (prs0, prs1)

                @blk.gpsimd
                def _(g):
                    ph_base = B.c("a_ph")
                    for k in range(KC):
                        par = k % 2
                        if k >= 2:
                            g.wait_ge(B.S("a_ph"), ph_base + k - 1)
                        B.inc(g.dma_start(
                            out=wstream[:, par * 2048:(par + 1) * 2048],
                            in_=wphi[k, :, :]), f"a_wl{par}", 16)

                @blk.tensor
                def _(t):
                    B.w(t, "a_rv")
                    base_wl = {0: B.c("a_wl0") - 16 * (KC // 2),
                               1: B.c("a_wl1") - 16 * (KC // 2)}
                    base_pf = B.c("a_pf")
                    base_cx = B.c("a_cx")
                    for k in range(KC):
                        par = k % 2
                        t.wait_ge(B.S(f"a_wl{par}"), base_wl[par] + 16 * (k // 2 + 1))
                        if k >= 2:
                            t.wait_ge(B.S("a_pf"), base_pf + k - 1)
                            t.wait_ge(B.S("a_cx"), base_cx + 2 * (k - 1))
                        last = None
                        B.g(t, 0)
                        for k2 in range(KC):
                            lh = wstream[:, par * 2048 + k2 * 128:par * 2048 + (k2 + 1) * 128]
                            for ts_i in range(2):
                                last = t.matmul(pph[par][:, ts_i * TS:(ts_i + 1) * TS],
                                                lh, hc(k2, ts_i * TS, (ts_i + 1) * TS),
                                                start=(k2 == 0), stop=(k2 == KC - 1))
                        B.inc(last, "a_ph")
                        last = None
                        for hh in range(2):
                            B.g(t, hh * 64)
                            n = 2 * k + hh
                            for ts_i in range(2):
                                last = t.matmul(
                                    prs[par][hh * 64:(hh + 1) * 64,
                                             ts_i * TS:(ts_i + 1) * TS],
                                    U_sb[hh * 64:hh * 64 + 64, n * 64:(n + 1) * 64],
                                    qa[hh * 64:(hh + 1) * 64,
                                       k * T + ts_i * TS:k * T + (ts_i + 1) * TS],
                                    start=True, stop=True)
                        B.inc(last, "a_rs")

                @blk.scalar
                def _(s):
                    base = B.c("a_ph") - KC
                    for k in range(KC):
                        par = k % 2
                        s.wait_ge(B.S("a_ph"), base + k + 1)
                        B.inc(s.copy(phi_scr[:, par * T:(par + 1) * T],
                                     pph[par][:, :]), "a_pf")

                @blk.vector
                def _(v):
                    base_rs = B.c("a_rs") - KC
                    base_pf = B.c("a_pf") - KC
                    for k in range(KC):
                        par = k % 2
                        v.wait_ge(B.S("a_rs"), base_rs + k + 1)
                        v.wait_ge(B.S("a_pf"), base_pf + k + 1)
                        for ts_i in range(2):
                            B.inc(v.scalar_tensor_tensor(
                                qa[:, k * T + ts_i * TS:k * T + (ts_i + 1) * TS],
                                prs[par][:, ts_i * TS:(ts_i + 1) * TS],
                                r_col[:, k:k + 1],
                                phi_scr[:, par * T + ts_i * TS:par * T + (ts_i + 1) * TS],
                                op0=ALU.mult, op1=ALU.mult), "a_cx")

            if KSUB < 9:
                es2.close()
                return None
            import os as _os
            if not int(_os.environ.get("KEEP6", "0")) and not int(_os.environ.get("REUSEPH", "0")):
                esp6.close()
            esp7 = ExitStack()
            if int(_os.environ.get("REUSEPH", "0")):
                pwo0, pwo1 = pph0, pph1
            else:
                pwo0 = esp7.enter_context(nc.psum_tensor("pwo0", [128, T], F32))
                pwo1 = esp7.enter_context(nc.psum_tensor("pwo1", [128, T], F32))
            if True:
                pwo = (pwo0, pwo1)
                t_ld = "a_wol"; t_fr = "a_wof"; t_pe = "a_wop"; t_ad = "x1"
                cx_total = B.c("a_cx")

                @blk.gpsimd
                def _(g):
                    g.wait_ge(B.S("a_ph"), B.c("a_ph"))
                    pe_base0 = B.c(t_pe)
                    for m in range(KC):
                        par = m % 2
                        if m >= 2:
                            g.wait_ge(B.S(t_pe), pe_base0 + m - 1)
                        B.inc(g.dma_start(out=wstream[:, par * 2048:(par + 1) * 2048],
                                          in_=wo[m, :, :]), t_ld + str(par), 16)

                ad_base = B.c(t_ad)

                @blk.tensor
                def _(t):
                    t.wait_ge(B.S("a_cx"), cx_total)
                    B.g(t, 0)
                    ld_base = {0: B.c(t_ld + "0") - 16 * (KC // 2),
                               1: B.c(t_ld + "1") - 16 * (KC // 2)}
                    for m in range(KC):
                        par = m % 2
                        t.wait_ge(B.S(t_ld + str(par)), ld_base[par] + (m // 2 + 1) * 16)
                        if m >= 2:
                            t.wait_ge(B.S(t_ad), ad_base + m - 1)
                        last = None
                        for k in range(KC):
                            lh = wstream[:, par * 2048 + k * 128:
                                         par * 2048 + (k + 1) * 128]
                            for ts_i in range(2):
                                last = t.matmul(
                                    pwo[par][:, ts_i * TS:(ts_i + 1) * TS], lh,
                                    qa[:, k * T + ts_i * TS:k * T + (ts_i + 1) * TS],
                                    start=(k == 0), stop=(k == KC - 1))
                        B.inc(last, t_pe)

                @blk.vector
                def _(v):
                    base_pe = B.c(t_pe) - KC
                    for m in range(KC):
                        par = m % 2
                        v.wait_ge(B.S(t_pe), base_pe + m + 1)
                        B.inc(v.scalar_tensor_tensor(xc(m), pwo[par][:, :], 1024.0,
                                                     xc(m), op0=ALU.mult,
                                                     op1=ALU.add), t_ad)
            esp7.close()
            if int(_os.environ.get("KEEP6", "0")) or int(_os.environ.get("REUSEPH", "0")):
                esp6.close()
            esp6s.close()
        es2.close()
        return "x1"

    def ph_ffn(tag, hn_gate):
        esf = ExitStack()
        tt = lambda *a, **k: esf.enter_context(nc.sbuf_tensor(*a, **k))
        gubuf = tt(f"gu_{tag}", [128, 2 * 4096], F16)
        dnbuf = tt(f"dn_{tag}", [128, 2 * 2816], F16)
        act2 = tt(f"act_{tag}", [128, 2 * NKI * T], F16)
        silu_s = tt(f"sil_{tag}", [128, 2 * T], F16)
        tmp_s = tt(f"tmp_{tag}", [128, 2 * T], F16)
        cbc = tt(f"cbc_{tag}", [128, 2 * T], F16)
        ctm = tt(f"ctm_{tag}", [128, 128], F16)
        combT = tt(f"cT_{tag}", [16, T], F16)
        crow = tt(f"crw_{tag}", [1, 2 * T], F16)
        sgrow = tt(f"sg_{tag}", [1, T], F16)
        prb = tt(f"prb_{tag}", [128, 16], F32)
        zap = tt(f"zap_{tag}", [128, 16], F32)
        mx8 = tt(f"mx8_{tag}", [128, 8], F32)
        sc4 = tt(f"sc4_{tag}", [128, 4], F32)

        r_pe = f"{tag}r_pe"; r_v = f"{tag}r_v"; r_s = f"{tag}r_s"
        r_ct = f"{tag}r_ct"; r_cv = f"{tag}r_cv"; r_sg = f"{tag}r_sg"
        r_dm = f"{tag}r_dm"
        rpe0 = B.c(r_pe)

        with nc.psum_tensor(f"prt0_{tag}", [128, 16], F32) as prt0, \
             nc.psum_tensor(f"prt1_{tag}", [128, 16], F32) as prt1, \
             nc.psum_tensor(f"pct_{tag}", [16, 128], F16) as pct, \
             nc.psum_tensor(f"psg_{tag}", [1, T], F32) as psg:
            prt = (prt0, prt1)

            @blk.tensor
            def _(t_):
                t_.wait_ge(B.S(hn_gate[0]), hn_gate[1])
                B.w(t_, "s_cn2")
                B.g(t_, 0)
                last = None
                for ts_i in range(2):
                    for k in range(KC):
                        last = t_.matmul(psg[:, ts_i * TS:(ts_i + 1) * TS],
                                         shgw_s[:, k:k + 1],
                                         hc(k, ts_i * TS, (ts_i + 1) * TS),
                                         start=(k == 0), stop=(k == KC - 1))
                B.inc(last, r_pe)
                base_s = B.c(r_s)
                for ti in range(8):
                    par = ti % 2
                    if ti >= 2:
                        t_.wait_ge(B.S(r_s), base_s + 2 * (ti - 2) + 1)
                    last = None
                    for k in range(KC):
                        last = t_.matmul(
                            prt[par][:, :],
                            hn_all[:, k * T + ti * 128:k * T + (ti + 1) * 128],
                            router_s[:, k * NE:(k + 1) * NE],
                            start=(k == 0), stop=(k == KC - 1))
                    B.inc(last, r_pe)
                base_v = B.c(r_v)
                base_cv = B.c(r_cv)
                for ti in range(8):
                    t_.wait_ge(B.S(r_v), base_v + 11 * (ti + 1))
                    if ti >= 1:
                        t_.wait_ge(B.S(r_cv), base_cv + ti)
                    B.inc(t_.transpose(pct[:, :], ctm[:, ti * 16:(ti + 1) * 16],
                                       ident16[:, :]), r_ct)

            @blk.scalar
            def _(s):
                s.wait_ge(B.S(r_pe), rpe0 + 1)
                B.inc(s.activation(sgrow[:, :], psg[:, :], AF.Sigmoid), r_sg)
                base_v = B.c(r_v)
                for ti in range(8):
                    par = ti % 2
                    s.wait_ge(B.S(r_pe), rpe0 + 1 + ti + 1)
                    s.wait_ge(B.S(r_v), base_v + 11 * ti + 1)
                    B.inc(s.activation(prb[:, :], prt[par][:, :], AF.Exp,
                                       bias=sc4[:, 0:1]), r_s)
                    s.wait_ge(B.S(r_v), base_v + 11 * ti + 5)
                    B.inc(s.memzero(mx8[:, 4:8]), r_s)
                base_ct = B.c(r_ct) - 8
                for ti in range(8):
                    s.wait_ge(B.S(r_ct), base_ct + ti + 1)
                    B.inc(s.copy(combT[0:16, ti * 128:(ti + 1) * 128], pct[:, :]),
                          r_cv)

            @blk.vector
            def _(v):
                base_pe = rpe0 + 1
                base_s = B.c(r_s) - 16
                for ti in range(8):
                    par = ti % 2
                    v.wait_ge(B.S(r_pe), base_pe + ti + 1)
                    if ti >= 1:
                        v.wait_ge(B.S(r_s), base_s + 2 * (ti - 1) + 1)
                    B.inc(v.reduce_max(sc4[:, 0:1], prt[par][:, :], axis=AX.X,
                                       negate=True), r_v)
                    v.wait_ge(B.S(r_s), base_s + 2 * ti + 1)
                    B.inc(v.reduce_sum(sc4[:, 1:2], prb[:, :], axis=AX.X), r_v)
                    B.w(v, r_v)
                    B.inc(v.reciprocal(sc4[:, 2:3], sc4[:, 1:2]), r_v)
                    B.w(v, r_v)
                    B.inc(v.tensor_scalar_mul(prb[:, :], prb[:, :], sc4[:, 2:3]),
                          r_v)
                    B.w(v, r_v)
                    B.inc(v.max(mx8[:, :], prb[:, :]), r_v)
                    v.wait_ge(B.S(r_s), base_s + 2 * ti + 2)
                    B.inc(v.match_replace(zap[:, :], mx8[:, :], prb[:, :], 0.0),
                          r_v)
                    B.w(v, r_v)
                    B.inc(v.tensor_sub(zap[:, :], prb[:, :], zap[:, :]), r_v)
                    B.w(v, r_v)
                    B.inc(v.reduce_sum(sc4[:, 3:4], zap[:, :], axis=AX.X), r_v)
                    B.w(v, r_v)
                    B.inc(v.tensor_scalar_max(sc4[:, 3:4], sc4[:, 3:4], 1e-9), r_v)
                    B.w(v, r_v)
                    B.inc(v.reciprocal(sc4[:, 3:4], sc4[:, 3:4]), r_v)
                    B.w(v, r_v)
                    B.inc(v.tensor_scalar_mul(ctm[:, ti * 16:(ti + 1) * 16],
                                              zap[:, :], sc4[:, 3:4]), r_v)

            @blk.gpsimd
            def _(g):
                B.w(g, r_sg)
                B.w(g, r_cv)
                B.inc(g.dma_start(out=combT_d[0:16, :], in_=combT[:, :]), r_dm, 16)
                for j in range(4):
                    B.inc(g.dma_start(out=combT_d[16 + j:17 + j, :],
                                      in_=sgrow[:, :]), r_dm, 16)

        with nc.psum_tensor(f"pg0_{tag}", [128, T], F32) as pg0, \
             nc.psum_tensor(f"pg1_{tag}", [128, T], F32) as pg1, \
             nc.psum_tensor(f"pu0_{tag}", [128, T], F32) as pu0, \
             nc.psum_tensor(f"pu1_{tag}", [128, T], F32) as pu1:
            pg = (pg0, pg1)
            pu = (pu0, pu1)
            f_gl = (f"{tag}gl0", f"{tag}gl1")
            f_dl = (f"{tag}dl0", f"{tag}dl1")
            f_cd = f"{tag}cd"
            f_cb = f"{tag}cb"
            f_as = f"{tag}as"
            f_tm = f"{tag}tm"
            f_aa = f"{tag}aa"
            f_bm = f"{tag}bm"
            f_dn = f"{tag}dn"
            f_rg = (f"{tag}rg0", f"{tag}rg1")
            f_ru = (f"{tag}ru0", f"{tag}ru1")
            for nm in list(f_gl) + list(f_dl) + [f_cd, f_cb, f_as, f_tm, f_aa,
                                                 f_bm, f_dn] + list(f_rg) + \
                    list(f_ru):
                B.S(nm)
            rcv_total = B.c(r_cv)

            @blk.gpsimd
            def _(g):
                g.wait_ge(B.S(r_cv), rcv_total)
                g.wait_ge(B.S(r_dm), B.c(r_dm))
                B.inc(g.nop(), f_as, 1)
                B.inc(g.nop(), f_cb, 2)
                import contextlib
                with contextlib.ExitStack() as erx:
                    regs = {}
                    for nm, init in (("rc", 0), ("rgu", 0), ("rdn", 0),
                                     ("was", 0), ("wcb", 0), ("wbm", 16)):
                        r = erx.enter_context(g.register("gr_" + nm))
                        g.reg_mov(r, init)
                        regs[nm] = r

                    def gu_dma(wpar):
                        g.wait_ge(B.S(f_as), regs["was"])
                        g.reg_alu(regs["was"], regs["was"], 1, ALU.add)
                        g.dma_start(
                            out=gubuf[:, wpar * 4096:(wpar + 1) * 4096],
                            in_=bass.AP(wgu, regs["rgu"], [[4096, 128], [1, 4096]])
                            ).then_inc(B.S(f_gl[wpar]), 16)
                        g.reg_alu(regs["rgu"], regs["rgu"], 128 * 4096, ALU.add)

                    def dn_dma(dpar):
                        g.wait_ge(B.S(f_bm), regs["wbm"])
                        g.reg_alu(regs["wbm"], regs["wbm"], 2, ALU.add)
                        g.dma_start(
                            out=dnbuf[:, dpar * 2816:(dpar + 1) * 2816],
                            in_=bass.AP(wdn, regs["rdn"],
                                        [[2 * IMOE, 128], [1, 2 * IMOE]])
                            ).then_inc(B.S(f_dl[dpar]), 16)
                        g.reg_alu(regs["rdn"], regs["rdn"], 2 * 128 * IMOE,
                                  ALU.add)

                    with g.Fori(0, NPASS // 2) as hh:
                        for par in range(2):
                            g.wait_ge(B.S(f_cb), regs["wcb"])
                            g.reg_alu(regs["wcb"], regs["wcb"], 2, ALU.add)
                            g.dma_start(
                                out=crow[0:1, par * T:(par + 1) * T],
                                in_=bass.AP(combT_d, regs["rc"], [[T, 1], [1, T]])
                                ).then_inc(B.S(f_cd), 16)
                            g.reg_alu(regs["rc"], regs["rc"], T, ALU.add)
                            with g.Fori(0, 5) as uu:
                                for u2 in range(2):
                                    gu_dma((par + u2) % 2)
                            gu_dma(par % 2)
                            with g.Fori(0, 4) as vv:
                                for v2 in range(2):
                                    dn_dma(v2)
                B.bump(f_cd, 16 * NPASS)
                B.bump(f_gl[0], 16 * NPASS * NKI // 2)
                B.bump(f_gl[1], 16 * (NPASS * NKI - NPASS * NKI // 2))
                B.bump(f_dl[0], 16 * NPASS * 4)
                B.bump(f_dl[1], 16 * NPASS * 4)

            @blk.tensor
            def _(t_):
                t_.wait_ge(B.S(r_cv), rcv_total)
                t_.wait_ge(B.S(r_sg), B.c(r_sg))
                B.g(t_, 0)
                regs = {}
                import contextlib
                with contextlib.ExitStack() as erx:
                    for nm, init in (("cd", 16), ("gl0", 16), ("gl1", 16),
                                     ("dl0", 16), ("dl1", 16), ("ru0", 7),
                                     ("ru1", 5), ("rg0", 0), ("rg1", 0),
                                     ("dnA", 1), ("dnB", 2), ("aa", 24)):
                        r = erx.enter_context(t_.register("tr_" + nm))
                        t_.reg_mov(r, init)
                        regs[nm] = r
                    with t_.Fori(0, NPASS // 2) as hh:
                        for par in range(2):
                            t_.wait_ge(B.S(f_cd), regs["cd"])
                            t_.reg_alu(regs["cd"], regs["cd"], 16, ALU.add)
                            t_.wait_ge(B.S(f_ru[0]), regs["ru0"])
                            t_.reg_alu(regs["ru0"], regs["ru0"], 1, ALU.add)
                            for ts_i in range(2):
                                t_.matmul(pu[0][:, ts_i * TS:(ts_i + 1) * TS],
                                          ones_r[:, :],
                                          crow[0:1, par * T + ts_i * TS:
                                               par * T + (ts_i + 1) * TS],
                                          start=True, stop=True
                                          ).then_inc(B.S(f_cb), 1)
                            t_.wait_ge(B.S(f_dn), regs["dnA"])
                            for i in range(NKI):
                                sp = i % 2
                                wp = (par + i) % 2
                                t_.wait_ge(B.S(f_gl[wp]), regs["gl" + str(wp)])
                                t_.reg_alu(regs["gl" + str(wp)],
                                           regs["gl" + str(wp)], 16, ALU.add)
                                t_.wait_ge(B.S(f_rg[sp]), regs["rg" + str(sp)])
                                t_.reg_alu(regs["rg" + str(sp)],
                                           regs["rg" + str(sp)], 1, ALU.add)
                                t_.wait_ge(B.S(f_ru[sp]), regs["ru" + str(sp)])
                                t_.reg_alu(regs["ru" + str(sp)],
                                           regs["ru" + str(sp)], 1, ALU.add)
                                last = None
                                for k in range(KC):
                                    lg = gubuf[:, wp * 4096 + k * 128:
                                               wp * 4096 + (k + 1) * 128]
                                    for ts_i in range(2):
                                        t_.matmul(
                                            pg[sp][:, ts_i * TS:(ts_i + 1) * TS],
                                            lg, hc(k, ts_i * TS, (ts_i + 1) * TS),
                                            start=(k == 0), stop=(k == KC - 1))
                                for k in range(KC):
                                    lu = gubuf[:, wp * 4096 + 2048 + k * 128:
                                               wp * 4096 + 2048 + (k + 1) * 128]
                                    for ts_i in range(2):
                                        last = t_.matmul(
                                            pu[sp][:, ts_i * TS:(ts_i + 1) * TS],
                                            lu, hc(k, ts_i * TS, (ts_i + 1) * TS),
                                            start=(k == 0), stop=(k == KC - 1))
                                last.then_inc(B.S(f_as), 1)
                            t_.wait_ge(B.S(f_aa), regs["aa"])
                            t_.reg_alu(regs["aa"], regs["aa"], 22, ALU.add)
                            for m in range(KC):
                                mp = m % 2
                                dpar = (m // 2) % 2
                                if mp == 0:
                                    t_.wait_ge(B.S(f_dl[dpar]),
                                               regs["dl" + str(dpar)])
                                    t_.reg_alu(regs["dl" + str(dpar)],
                                               regs["dl" + str(dpar)], 16,
                                               ALU.add)
                                if m < 2:
                                    t_.wait_ge(B.S(f_rg[mp]),
                                               regs["rg" + str(mp)])
                                else:
                                    t_.wait_ge(B.S(f_dn), regs["dnB"])
                                    t_.reg_alu(regs["dnB"], regs["dnB"], 1,
                                               ALU.add)
                                last = None
                                for ki in range(NKI):
                                    ld = dnbuf[:, dpar * 2816 + mp * IMOE +
                                               ki * 128:
                                               dpar * 2816 + mp * IMOE +
                                               (ki + 1) * 128]
                                    for ts_i in range(2):
                                        last = t_.matmul(
                                            pg[mp][:, ts_i * TS:(ts_i + 1) * TS],
                                            ld,
                                            act2[:, (par * NKI + ki) * T +
                                                 ts_i * TS:
                                                 (par * NKI + ki) * T +
                                                 (ts_i + 1) * TS],
                                            start=(ki == 0), stop=(ki == NKI - 1))
                                last.then_inc(B.S(f_bm), 1)
                            t_.reg_alu(regs["dnA"], regs["dnA"], 16, ALU.add)
                            t_.reg_alu(regs["dnB"], regs["dnB"], 2, ALU.add)
                B.bump(f_cb, 2 * NPASS)
                B.bump(f_as, NKI * NPASS)
                B.bump(f_bm, KC * NPASS)

            @blk.scalar
            def _(s):
                B.inc(s.nop(), f_ru[0], 7)
                B.inc(s.nop(), f_ru[1], 5)
                regs = {}
                import contextlib
                with contextlib.ExitStack() as erx:
                    for nm, init in (("cb", 4), ("ru0", 0), ("ru1", 0),
                                     ("as_", 2), ("aa", 0)):
                        r = erx.enter_context(s.register("sr_" + nm))
                        s.reg_mov(r, init)
                        regs[nm] = r
                    with s.Fori(0, NPASS // 2) as hh:
                        for par in range(2):
                            s.wait_ge(B.S(f_cb), regs["cb"])
                            s.reg_alu(regs["cb"], regs["cb"], 2, ALU.add)
                            s.wait_ge(B.S(f_ru[0]), regs["ru0"])
                            s.reg_alu(regs["ru0"], regs["ru0"], 7, ALU.add)
                            s.wait_ge(B.S(f_ru[1]), regs["ru1"])
                            s.reg_alu(regs["ru1"], regs["ru1"], 5, ALU.add)
                            s.activation(cbc[:, par * T:(par + 1) * T],
                                         pu[0][:, :], AF.Copy
                                         ).then_inc(B.S(f_ru[0]), 1)
                            for i in range(NKI):
                                sp = i % 2
                                s.wait_ge(B.S(f_as), regs["as_"])
                                s.reg_alu(regs["as_"], regs["as_"], 1, ALU.add)
                                s.wait_ge(B.S(f_aa), regs["aa"])
                                s.reg_alu(regs["aa"], regs["aa"], 2, ALU.add)
                                s.activation(silu_s[:, sp * T:sp * T + TS],
                                             pg[sp][:, 0:TS], AF.Silu)
                                s.activation(silu_s[:, sp * T + TS:(sp + 1) * T],
                                             pg[sp][:, TS:T], AF.Silu
                                             ).then_inc(B.S(f_rg[sp]), 1)
                B.bump(f_ru[0], NPASS)
                B.bump(f_rg[0], 6 * NPASS)
                B.bump(f_rg[1], 5 * NPASS)

            @blk.vector
            def _(v):
                v.engine_nop().then_inc(B.S(f_bm), 18)
                B.bump(f_bm, 18)
                v.engine_nop().then_inc(B.S(f_dn), 1)
                B.bump(f_dn, 1)
                v.engine_nop().then_inc(B.S(f_aa), 2)
                B.bump(f_aa, 2)
                regs = {}
                import contextlib
                with contextlib.ExitStack() as erx:
                    for nm, init in (("as_", 2), ("cr", 8), ("rg0", 1),
                                     ("rg1", 1), ("bm", 2), ("tm", 1),
                                     ("ruA0", 9), ("ruA1", 6), ("bme", 19),
                                     ("xo", 0)):
                        r = erx.enter_context(v.register("vr_" + nm))
                        v.reg_mov(r, init)
                        regs[nm] = r
                    with v.Fori(0, NPASS // 2) as hh:
                        for par in range(2):
                            v.wait_ge(B.S(f_ru[0]), regs["cr"])
                            for i in range(NKI):
                                sp = i % 2
                                v.wait_ge(B.S(f_as), regs["as_"])
                                v.reg_alu(regs["as_"], regs["as_"], 1, ALU.add)
                                v.tensor_mul(
                                    tmp_s[:, sp * T:sp * T + TS],
                                    pu[sp][:, 0:TS],
                                    cbc[:, par * T:par * T + TS]
                                    ).then_inc(B.S(f_tm), 1)
                                v.tensor_mul(
                                    tmp_s[:, sp * T + TS:(sp + 1) * T],
                                    pu[sp][:, TS:T],
                                    cbc[:, par * T + TS:(par + 1) * T]
                                    ).then_inc(B.S(f_ru[sp]), 1)
                                v.wait_ge(B.S(f_rg[sp]), regs["rg" + str(sp)])
                                v.reg_alu(regs["rg" + str(sp)],
                                          regs["rg" + str(sp)], 1, ALU.add)
                                v.wait_ge(B.S(f_bm), regs["bm"])
                                v.wait_ge(B.S(f_tm), regs["tm"])
                                v.reg_alu(regs["tm"], regs["tm"], 1, ALU.add)
                                v.wait_ge(B.S(f_ru[sp]),
                                          regs["ruA" + str(sp)])
                                v.reg_alu(regs["ruA" + str(sp)],
                                          regs["ruA" + str(sp)], 1, ALU.add)
                                v.tensor_mul(
                                    act2[:, (par * NKI + i) * T:
                                         (par * NKI + i) * T + TS],
                                    silu_s[:, sp * T:sp * T + TS],
                                    tmp_s[:, sp * T:sp * T + TS]
                                    ).then_inc(B.S(f_aa), 1)
                                v.tensor_mul(
                                    act2[:, (par * NKI + i) * T + TS:
                                         (par * NKI + i + 1) * T],
                                    silu_s[:, sp * T + TS:(sp + 1) * T],
                                    tmp_s[:, sp * T + TS:(sp + 1) * T]
                                    ).then_inc(B.S(f_aa), 1)
                            v.reg_alu(regs["cr"], regs["cr"], 7, ALU.add)
                            v.reg_alu(regs["bm"], regs["bm"], 16, ALU.add)
                            v.reg_alu(regs["ruA0"], regs["ruA0"], 1, ALU.add)
                            for m in range(KC):
                                mp = m % 2
                                v.wait_ge(B.S(f_bm), regs["bme"])
                                v.reg_alu(regs["bme"], regs["bme"], 1,
                                          ALU.add)
                                v.tensor_add(xc(m), xc(m), pg[mp][:, :]
                                             ).then_inc(B.S(f_dn), 1)
                B.bump(f_tm, NKI * NPASS)
                B.bump(f_ru[0], 6 * NPASS)
                B.bump(f_ru[1], 5 * NPASS)
                B.bump(f_aa, 2 * NKI * NPASS)
                B.bump(f_dn, KC * NPASS)
        esf.close()
        return f_dn

    def ph_dbg_hn(gate):
        @blk.gpsimd
        def _(g):
            g.wait_ge(B.S(gate[0]), gate[1])
            for k in range(KC):
                B.inc(g.dma_start(out=DBG[k, :, :], in_=hc(k)), "o_st", 16)

    def ph_out(gate):
        with nc.sbuf_tensor("ot", [128, 2 * D], F32) as ot, \
             nc.psum_tensor("pot0", [128, 128], F32) as pot0, \
             nc.psum_tensor("pot1", [128, 128], F32) as pot1:
            pot = (pot0, pot1)
            base_pe = B.c("o_pe")
            base_ev = B.c("o_ev")

            @blk.tensor
            def _(t):
                t.wait_ge(B.S(gate[0]), gate[1])
                for i in range(8):
                    for k in range(KC):
                        idx = i * KC + k
                        if idx >= 2:
                            t.wait_ge(B.S("o_ev"), base_ev + idx - 1)
                        B.inc(t.transpose(pot[idx % 2][:, :],
                                          xc(k, i * 128, (i + 1) * 128),
                                          identf[:, :]), "o_pe")

            @blk.scalar
            def _(s):
                for i in range(8):
                    par = i % 2
                    if i >= 2:
                        s.wait_ge(B.S(f"o_st{par}"), 16 * (i // 2))
                    for k in range(KC):
                        idx = i * KC + k
                        s.wait_ge(B.S("o_pe"), base_pe + idx + 1)
                        B.inc(s.copy(ot[:, par * D + k * 128: par * D + (k + 1) * 128],
                                     pot[idx % 2][:, :]), "o_ev")

            @blk.sync
            def _(sy):
                for i in range(8):
                    par = i % 2
                    sy.wait_ge(B.S("o_ev"), base_ev + (i + 1) * KC)
                    B.inc(sy.dma_start(OUT[i * 128:(i + 1) * 128, :],
                                       ot[:, par * D:(par + 1) * D]), f"o_st{par}", 16)
                B.w(sy, "o_st0")
                B.w(sy, "o_st1")

    ph_setup()
    x_gate = ("s_x", 16 * KC)
    if stage >= 1:
        ln1 = ph_ln("ln1", None)
        import os
        if os.environ.get("KDBG"):
            print("[sbuf] after ln1 scope:", nc.sbuf_bytes_remaining)
        xs = ph_attn_a(ln1)
        if xs is not None:
            x_gate = (xs, B.c(xs))
        if debug and xs is not None:
            ph_dbg_hn(x_gate)
    if stage >= 2 and x_gate[0] != "s_x":
        ln2 = ph_ln("ln2", x_gate, sb=64.0, sq=16384.0, sr=1024.0)
        f1 = ph_ffn("f1", (ln2, B.c(ln2)))
        x_gate = (f1, B.c(f1))
    ph_out(x_gate)

    es.close()
    nc.finalize()
    return nc


def kernel(**inputs):
    per_core = _prepare(inputs)
    nc = _build(STAGE, DEBUG)
    res = run_bass_kernel_spmd(nc, per_core, core_ids=list(range(NC)), trace=TRACE)
    if DEBUG:
        kernel.dbg = [res.results[c].get("dbg") for c in range(NC)]
    outs = [res.results[c]["out"] for c in range(NC)]
    return np.concatenate(outs, 0).reshape(1, NC * T, D)
